# version 04 (final)
# baseline (speedup 1.0000x reference)
import numpy as np
import ml_dtypes
from contextlib import ExitStack

import concourse.bass as bass
import concourse.mybir as mybir
from concourse.bass_isa import ReduceOp
from concourse import library_config

F32 = mybir.dt.float32
BF16 = mybir.dt.bfloat16
AF = mybir.ActivationFunctionType
ALU = mybir.AluOpType
AX = mybir.AxisListType.X

NCORES = 8
B = 2
D = 1024
NH = 16
NT = 16
TOK = NT * 128
NE = 32
EPS = 1e-5
RELU = 4608
K0 = 8064
RC = 12582912.0
INV_AL = 1.0 / 1.702

ENG_ATTR = {"sy": "sync", "t": "tensor", "v": "vector", "s": "scalar", "g": "gpsimd"}


class Prog:
    def __init__(self, nc, sems):
        self.nc = nc
        self.sems = sems
        self.streams = {e: [] for e in ENG_ATTR}
        self.cnt = {k: 0 for k in sems}

    def op(self, eng, fn, inc=None, n=1):
        self.streams[eng].append((fn, inc, n))
        if inc is not None:
            self.cnt[inc] += n
            return self.cnt[inc]
        return None

    def dma(self, eng, out, in_, inc, **kw):
        self.op(eng, lambda e, out=out, in_=in_, kw=kw: e.dma_start(out=out, in_=in_, **kw),
                inc=inc, n=16)
        return self.cnt[inc]

    def wait(self, eng, sem, val):
        if val > 0:
            self.streams[eng].append((("W", sem, val), None, 0))

    def drain(self, eng):
        self.streams[eng].append((("D",), None, 0))

    def flush(self, name=None):
        nc = self.nc
        with nc.Block(name) as blk:
            for ename in ENG_ATTR:
                stream = self.streams[ename]
                if not stream:
                    continue

                def mk(stream=list(stream)):
                    def run(eng):
                        for fn, inc, n in stream:
                            if isinstance(fn, tuple):
                                if fn[0] == "W":
                                    eng.wait_ge(self.sems[fn[1]], fn[2])
                                else:
                                    eng.drain()
                            else:
                                inst = fn(eng)
                                if inc is not None:
                                    inst.then_inc(self.sems[inc], n)
                    return run

                getattr(blk, ENG_ATTR[ename])(mk())
        self.streams = {e: [] for e in ENG_ATTR}


def bias_segments():
    by_j = {}
    for jpos in range(64):
        r, kj = jpos // 8, jpos % 8
        j0 = (r + 8 * kj) * 128
        for k in range(8):
            if j0 - k * 1024 >= 3971:
                by_j.setdefault(jpos, []).append((k, K0 - j0 + k * 1024))
    return by_j


def build(phases=4, taps=(), l1sub=6, l1f=4):
    nc = bass.Bass("TRN2", target_bir_lowering=False)
    taps = set(taps)

    h = {}
    specs = [("x_own", [NT, 128, D], F32),
             ("wq", [8, 128, D], F32), ("wk", [8, 128, D], F32), ("wv", [8, 128, D], F32),
             ("bq", [8, 128, 1], F32), ("bk", [8, 128, 1], F32), ("bv", [8, 128, 1], F32),
             ("relT", [NH, 128, RELU], F32), ("chv", [NH, 1], F32),
             ("gscale", [8, 128, 1], F32), ("gwT", [8, 128, NE], F32), ("gb", [NE, 1], F32),
             ("w1g", [NE, 8, 128, D], F32), ("w1l", [NE, 8, 128, D], F32),
             ("w2T", [NE, 8, 128, D], F32),
             ("b1g", [128, NE * 8], F32), ("b1l", [128, NE * 8], F32), ("b2", [NE, D], F32),
             ("wq1", [8, 128, 512], F32), ("wk1", [8, 128, 512], F32), ("wv1", [8, 128, 512], F32),
             ("wo1", [4, 128, D], F32), ("bo1", [1, D], F32),
             ("wf1", [8, 128, D], F32), ("bf1", [1, D], F32),
             ("ident_f", [128, 128], F32), ("ident_b", [128, 128], BF16)]
    for nm, sh, dt in specs:
        h[nm] = nc.dram_tensor(nm, sh, dt, kind="ExternalInput").ap()

    y = nc.dram_tensor("y_own", [NT, 128, D], F32, kind="ExternalOutput").ap()

    kv_in = nc.dram_tensor("kv_in", [2, D, TOK], BF16)
    kv_out = nc.dram_tensor("kv_out", [NCORES, 2, D, TOK], BF16)
    xsp = nc.dram_tensor("xsp", [128, NT * D], F32)
    qtsp = nc.dram_tensor("qtsp", [128, 8 * TOK], BF16)
    moe_hbm = nc.dram_tensor("moe_hbm", [NT, 128, D], F32)
    rowb = nc.dram_tensor("rowb", [16, 128], F32)
    cspb = nc.dram_tensor("cspb", [NE, TOK], BF16)
    xqsp = nc.dram_tensor("xqsp", [NT, 128, D], BF16)
    wqt_sp = nc.dram_tensor("wqt_sp", [128, 8 * D], BF16)
    nmsp = nc.dram_tensor("nmsp", [1, 1], F32)
    ar_in = nc.dram_tensor("ar_in", [128, 520], F32)
    ar_out = nc.dram_tensor("ar_out", [128, 520], F32)

    tap_specs = {"xnT": ([128, 8 * TOK], BF16), "tT": ([128, 8 * TOK], BF16),
                 "qT": ([128, 8 * TOK], BF16), "kv": ([2, D, TOK], BF16),
                 "gtok": ([128, NT * NE], F32), "ctok": ([128, NT * NE], F32),
                 "accT": ([128, 8 * TOK], F32), "moe": ([NT, 128, D], F32),
                 "x1": ([NT, 128, D], F32), "xn2T": ([128, 8 * TOK], BF16),
                 "ek": ([128, NT * 512], BF16), "qsT": ([128, 4 * TOK], BF16),
                 "ctx": ([128, 520], F32), "o1T": ([128, 4 * TOK], BF16),
                 "wqT1": ([128, 8 * D], BF16), "dbg": ([128, 2 * 4224 + 8192 + 2048], BF16), "dbg2": ([128, 1026], F32), "dbg3": ([128, 1024], F32)}
    tap_t = {t: nc.dram_tensor("tap_" + t, *tap_specs[t], kind="ExternalOutput").ap()
             for t in taps}

    segs_by_j = bias_segments()

    ctx = ExitStack()
    with ctx:
        sems = {}
        for s in ["ldc", "lx0", "lgw", "lb", "lpw", "lrel", "lch", "lkt", "lvt", "lcb",
                  "lr0", "lr1", "lmc", "lxr", "lqr", "lwf", "lfb", "lwo", "lwq",
                  "lxq0", "lxq1", "lrow", "lar", "st0", "st1", "stc", "stx", "sxq",
                  "swq0", "swq1", "mo0", "mo1", "mb0", "mb1", "srow", "sar", "sty",
                  "stp", "svp", "snm", "lnm", "mm", "vv", "ss", "cc", "sg"]:
            sems[s] = ctx.enter_context(nc.semaphore(s))
        P = Prog(nc, sems)

        ident_f = ctx.enter_context(nc.sbuf_tensor("identf", [128, 128], F32))
        ident_b = ctx.enter_context(nc.sbuf_tensor("identb", [128, 128], BF16))
        ones_b = ctx.enter_context(nc.sbuf_tensor("onesb", [128, 1], BF16))
        mu_sb = ctx.enter_context(nc.sbuf_tensor("mu", [128, NT], F32))
        rstd_sb = ctx.enter_context(nc.sbuf_tensor("rstd", [128, NT], F32))
        rho_sb = ctx.enter_context(nc.sbuf_tensor("rho", [128, NT], F32))
        sm1 = ctx.enter_context(nc.sbuf_tensor("sm1", [128, NT], F32))
        sm2 = ctx.enter_context(nc.sbuf_tensor("sm2", [128, NT], F32))
        xsc_sb = ctx.enter_context(nc.sbuf_tensor("xsc", [128, NT], F32))
        sca = ctx.enter_context(nc.sbuf_tensor("sca", [128, 1], F32))
        scb = ctx.enter_context(nc.sbuf_tensor("scb", [128, 1], F32))
        scc = ctx.enter_context(nc.sbuf_tensor("scc", [128, 1], F32))

        P.dma("sy", ident_f[:, :], h["ident_f"], "ldc")
        P.dma("sy", ident_b[:, :], h["ident_b"], "ldc")
        ident_ld = P.cnt["ldc"]
        P.op("v", lambda v: v.memset(ones_b[:, :], 1.0), inc="vv")

        def ln_tile(x_src, m, tmpf, do_rho):
            P.op("v", lambda v: v.tensor_reduce(sm1[:, m:m + 1], x_src, axis=AX, op=ALU.add), inc="vv")
            P.wait("s", "vv", P.cnt["vv"])
            P.op("s", lambda s: s.activation(tmpf[:, :], x_src, AF.Square), inc="ss")
            P.wait("v", "ss", P.cnt["ss"])
            P.op("v", lambda v: v.tensor_reduce(sm2[:, m:m + 1], tmpf[:, :], axis=AX, op=ALU.add))
            P.drain("v")
            P.op("v", lambda v: v.tensor_scalar(mu_sb[:, m:m + 1], sm1[:, m:m + 1], 1.0 / D, None, ALU.mult))
            P.op("v", lambda v: v.tensor_scalar(sm2[:, m:m + 1], sm2[:, m:m + 1], 1.0 / D, None, ALU.mult))
            P.drain("v")
            P.op("v", lambda v: v.tensor_mul(sm1[:, m:m + 1], mu_sb[:, m:m + 1], mu_sb[:, m:m + 1]))
            P.drain("v")
            P.op("v", lambda v: v.tensor_sub(sm2[:, m:m + 1], sm2[:, m:m + 1], sm1[:, m:m + 1]))
            P.drain("v")
            P.op("v", lambda v: v.tensor_scalar(sm1[:, m:m + 1], sm2[:, m:m + 1], EPS, None, ALU.add), inc="vv")
            P.wait("s", "vv", P.cnt["vv"])
            P.op("s", lambda s: s.sqrt(sca[:, :], sm1[:, m:m + 1]), inc="ss")
            P.wait("v", "ss", P.cnt["ss"])
            P.op("v", lambda v: v.reciprocal(rstd_sb[:, m:m + 1], sca[:, :]), inc="vv")
            P.drain("v")
            if do_rho:
                P.op("v", lambda v: v.tensor_mul(scb[:, :], rstd_sb[:, m:m + 1], rstd_sb[:, m:m + 1]))
                P.drain("v")
                P.op("v", lambda v: v.tensor_mul(scb[:, :], scb[:, :], sm2[:, m:m + 1]))
                P.drain("v")
                P.op("v", lambda v: v.tensor_scalar(scb[:, :], scb[:, :], EPS, None, ALU.add), inc="vv")
                P.wait("s", "vv", P.cnt["vv"])
                P.op("s", lambda s: s.sqrt(scc[:, :], scb[:, :]), inc="ss")
                P.wait("v", "ss", P.cnt["ss"])
                P.op("v", lambda v: v.reciprocal(rho_sb[:, m:m + 1], scc[:, :]), inc="vv")
                P.drain("v")

        def row_broadcast(col_ap, width, bc_sb, ps_f32, tmp_sb):
            P.drain("v")
            P.op("v", lambda v: v.engine_nop(), inc="vv")
            P.wait("t", "vv", P.cnt["vv"])
            P.wait("t", "ldc", ident_ld)
            P.op("t", lambda t: t.transpose(ps_f32[0:width, 0:128], col_ap, ident_f[:, :]), inc="mm")
            P.wait("v", "mm", P.cnt["mm"])
            P.op("v", lambda v: v.tensor_copy(tmp_sb[0:width, 0:128], ps_f32[0:width, 0:128]), inc="vv")
            P.wait("sy", "vv", P.cnt["vv"])
            P.wait("sy", "lrow", P.cnt["lrow"])
            P.dma("sy", rowb.ap()[0:width, :], tmp_sb[0:width, 0:128], "srow")
            P.wait("sy", "srow", P.cnt["srow"])
            P.dma("sy", bc_sb[:, :],
                  rowb.ap()[0:width, :].rearrange("a b -> (a b)").unsqueeze(0)
                  .partition_broadcast(128)[:, 0, :], "lrow")
            return P.cnt["lrow"]

        DMA_SEMS = ["ldc", "lx0", "lgw", "lb", "lpw", "lrel", "lch", "lkt", "lvt", "lcb",
                    "lr0", "lr1", "lmc", "lxr", "lqr", "lwf", "lfb", "lwo", "lwq",
                    "lxq0", "lxq1", "lrow", "lar", "st0", "st1", "stc", "stx", "sxq",
                    "swq0", "swq1", "mo0", "mo1", "mb0", "mb1", "srow", "sar", "sty",
                    "stp", "svp", "snm", "lnm"]

        def quiesce():
            for eng in ENG_ATTR:
                for s in DMA_SEMS:
                    P.wait(eng, s, P.cnt[s])

        pAB = ExitStack()
        tT = pAB.enter_context(nc.sbuf_tensor("tT", [128, 8 * TOK], BF16))
        cTb = pAB.enter_context(nc.sbuf_tensor("cTb", [NE, TOK], BF16))

        p0 = ExitStack()
        x_sb = p0.enter_context(nc.sbuf_tensor("x0", [128, NT * D], F32))
        xnT = p0.enter_context(nc.sbuf_tensor("xnT", [128, 8 * TOK], BF16))
        w_buf = p0.enter_context(nc.sbuf_tensor("wbuf", [128, 8 * D], BF16))
        gwT_sb = p0.enter_context(nc.sbuf_tensor("gwTs", [128, 8 * NE], BF16))
        bq_sb = p0.enter_context(nc.sbuf_tensor("bqs", [128, 8], F32))
        bk_sb = p0.enter_context(nc.sbuf_tensor("bks", [128, 8], F32))
        bv_sb = p0.enter_context(nc.sbuf_tensor("bvs", [128, 8], F32))
        gam_sb = p0.enter_context(nc.sbuf_tensor("gams", [128, 8], F32))
        gb_sb = p0.enter_context(nc.sbuf_tensor("gbs", [NE, 1], F32))
        tmpa = p0.enter_context(nc.sbuf_tensor("tmpa", [128, D], F32))
        tmpb = p0.enter_context(nc.sbuf_tensor("tmpb", [128, D], BF16))
        stg = p0.enter_context(nc.sbuf_tensor("stg", [128, 2 * 512], BF16))
        rho_bc = p0.enter_context(nc.sbuf_tensor("rhobc", [128, TOK], F32))
        gT_sb = p0.enter_context(nc.sbuf_tensor("gTs", [NE, TOK], F32))
        gtok = p0.enter_context(nc.sbuf_tensor("gtok", [128, NT * NE], F32))
        ctok = p0.enter_context(nc.sbuf_tensor("ctok", [128, NT * NE], F32))
        cT_sb = p0.enter_context(nc.sbuf_tensor("cTs", [NE, TOK], F32))
        mxs = p0.enter_context(nc.sbuf_tensor("mxs", [128, 4], F32))
        eqs = p0.enter_context(nc.sbuf_tensor("eqs", [128, 4 * NE], F32))
        gg1 = p0.enter_context(nc.sbuf_tensor("gg1", [128, NE], F32))
        gg2 = p0.enter_context(nc.sbuf_tensor("gg2", [128, NE], F32))
        dd = p0.enter_context(nc.sbuf_tensor("dd", [128, 3], F32))
        ee = p0.enter_context(nc.sbuf_tensor("ee", [128, 3], F32))
        acc4 = p0.enter_context(nc.sbuf_tensor("acc4", [128, NE], F32))

        ps_a = p0.enter_context(nc.psum_tensor("ps_a", [128, 512], F32))
        ps_b = p0.enter_context(nc.psum_tensor("ps_b", [128, 512], F32))
        ps_t = p0.enter_context(nc.psum_tensor("ps_t", [128, 128], BF16))
        ps_t2 = p0.enter_context(nc.psum_tensor("ps_t2", [128, 128], BF16))
        ps_f = p0.enter_context(nc.psum_tensor("ps_f", [128, 512], F32))

        for i in range(NT):
            P.dma("sy", x_sb[:, i * D:(i + 1) * D], h["x_own"][i], "lx0")
        for dt in range(8):
            P.dma("g", gwT_sb[:, dt * NE:(dt + 1) * NE], h["gwT"][dt], "lgw")
        for nm, sb in [("bq", bq_sb), ("bk", bk_sb), ("bv", bv_sb), ("gscale", gam_sb)]:
            for dt in range(8):
                P.dma("sy", sb[:, dt:dt + 1], h[nm][dt], "lb")
        P.dma("sy", gb_sb[:, :], h["gb"], "lb")
        x_ld = P.cnt["lx0"]
        b_ld = P.cnt["lb"]

        P.wait("v", "lx0", x_ld)
        P.wait("v", "lb", b_ld)
        P.wait("s", "lx0", x_ld)
        pst_ev = {id(ps_t): 0, id(ps_t2): 0}
        last_tr_mm = 0
        for m in range(NT):
            xs = x_sb[:, m * D:(m + 1) * D]
            ln_tile(xs, m, tmpa, do_rho=True)
            P.op("v", lambda v, xs=xs, m=m: v.tensor_scalar(
                tmpa[:, :], xs, mu_sb[:, m:m + 1], rstd_sb[:, m:m + 1], ALU.subtract, ALU.mult))
            P.drain("v")
            P.wait("v", "mm", last_tr_mm)
            P.op("v", lambda v: v.tensor_copy(tmpb[:, :], tmpa[:, :]), inc="vv")
            P.op("v", lambda v, xs=xs: v.tensor_add(xs, xs, tmpa[:, :]))
            P.drain("v")
            P.wait("t", "vv", P.cnt["vv"])
            P.wait("t", "ldc", ident_ld)
            for dt in range(8):
                pst = ps_t if dt % 2 == 0 else ps_t2
                P.wait("t", "vv", pst_ev[id(pst)])
                P.op("t", lambda t, dt=dt, pst=pst: t.transpose(
                    pst[:, :], tmpb[:, dt * 128:(dt + 1) * 128], ident_b[:, :]), inc="mm")
                P.wait("v", "mm", P.cnt["mm"])
                P.op("v", lambda v, dt=dt, m=m, pst=pst: v.tensor_copy(
                    xnT[:, dt * TOK + m * 128: dt * TOK + (m + 1) * 128], pst[:, :]), inc="vv")
                pst_ev[id(pst)] = P.cnt["vv"]
            last_tr_mm = P.cnt["mm"]
        xnT_vv = P.cnt["vv"]

        rho_ld = row_broadcast(rho_sb[:, :], NT, rho_bc, ps_f, tmpa)
        P.wait("v", "lrow", rho_ld)
        for dt in range(8):
            P.op("v", lambda v, dt=dt: v.tensor_scalar(
                tT[:, dt * TOK:(dt + 1) * TOK], xnT[:, dt * TOK:(dt + 1) * TOK],
                gam_sb[:, dt:dt + 1], None, ALU.mult))
            P.drain("v")
            P.op("v", lambda v, dt=dt: v.tensor_mul(
                tT[:, dt * TOK:(dt + 1) * TOK], tT[:, dt * TOK:(dt + 1) * TOK], rho_bc[:, :]), inc="vv")
            P.drain("v")
        tT_vv = P.cnt["vv"]

        P.op("v", lambda v: v.tensor_scalar(bq_sb[:, :], bq_sb[:, :], 0.125, None, ALU.mult), inc="vv")
        P.drain("v")
        bqq_vv = P.cnt["vv"]

        P.wait("t", "vv", xnT_vv)
        ps_ev = {id(ps_a): 0, id(ps_b): 0}
        stg_st = {0: 0, 1: 0}
        w_lg = 0
        ci = 0
        for which, wname, b_sb in [("q", "wq", bq_sb), ("k", "wk", bk_sb), ("v", "wv", bv_sb)]:
            P.wait("g", "mm", P.cnt["mm"])
            for dt in range(8):
                w_lg = P.dma("g", w_buf[:, dt * D:(dt + 1) * D], h[wname][dt], "lpw")
            P.wait("t", "lpw", w_lg)
            for ht in range(8):
                for ncn in range(4):
                    ps = ps_a if ci % 2 == 0 else ps_b
                    P.wait("t", "vv", ps_ev[id(ps)])
                    for dt in range(8):
                        P.op("t", lambda t, ht=ht, ncn=ncn, dt=dt, ps=ps: t.matmul(
                            ps[:, :], w_buf[:, dt * D + ht * 128: dt * D + (ht + 1) * 128],
                            xnT[:, dt * TOK + ncn * 512: dt * TOK + (ncn + 1) * 512],
                            start=(dt == 0), stop=(dt == 7)), inc="mm" if dt == 7 else None)
                    P.wait("v", "mm", P.cnt["mm"])
                    slot = ci % 2
                    ssem = "st0" if slot == 0 else "st1"
                    P.wait("v", ssem, stg_st[slot])
                    sslice = stg[:, slot * 512:(slot + 1) * 512]
                    if which == "q":
                        P.wait("v", "vv", bqq_vv)
                        P.op("v", lambda v, ht=ht, ps=ps, sslice=sslice: v.tensor_scalar(
                            sslice, ps[:, :], 0.125, bq_sb[:, ht:ht + 1], ALU.mult, ALU.add), inc="vv")
                        P.wait("sy", "vv", P.cnt["vv"])
                        P.dma("sy", qtsp.ap()[:, ht * TOK + ncn * 512: ht * TOK + (ncn + 1) * 512],
                              sslice, ssem)
                    else:
                        P.op("v", lambda v, b_sb=b_sb, ht=ht, ps=ps, sslice=sslice: v.tensor_scalar(
                            sslice, ps[:, :], b_sb[:, ht:ht + 1], None, ALU.add), inc="vv")
                        P.wait("sy", "vv", P.cnt["vv"])
                        kvi = 0 if which == "k" else 1
                        P.dma("sy", kv_in.ap()[kvi, ht * 128:(ht + 1) * 128, ncn * 512:(ncn + 1) * 512],
                              sslice, ssem)
                    stg_st[slot] = P.cnt[ssem]
                    ps_ev[id(ps)] = P.cnt["vv"]
                    ci += 1
        kv_st0, kv_st1 = P.cnt["st0"], P.cnt["st1"]

        P.wait("g", "st0", kv_st0)
        P.wait("g", "st1", kv_st1)
        P.op("g", lambda g: g.collective_compute(
            "AllGather", ALU.bypass, replica_groups=[list(range(NCORES))],
            ins=[kv_in.ap()], outs=[kv_out.ap()]), inc="cc")

        P.wait("t", "vv", tT_vv)
        P.wait("t", "lgw", P.cnt["lgw"])
        for ncn in range(4):
            ps = ps_a if ncn % 2 == 0 else ps_b
            P.wait("t", "vv", ps_ev[id(ps)])
            for dt in range(8):
                P.op("t", lambda t, ncn=ncn, dt=dt, ps=ps: t.matmul(
                    ps[0:NE, :], gwT_sb[:, dt * NE:(dt + 1) * NE],
                    tT[:, dt * TOK + ncn * 512: dt * TOK + (ncn + 1) * 512],
                    start=(dt == 0), stop=(dt == 7)), inc="mm" if dt == 7 else None)
            P.wait("v", "mm", P.cnt["mm"])
            P.op("v", lambda v, ncn=ncn, ps=ps: v.tensor_scalar(
                gT_sb[:, ncn * 512:(ncn + 1) * 512], ps[0:NE, :], gb_sb[:, 0:1], None, ALU.add), inc="vv")
            ps_ev[id(ps)] = P.cnt["vv"]
        gT_vv = P.cnt["vv"]

        fev = {id(ps_f): 0}
        P.wait("t", "vv", gT_vv)
        for m in range(NT):
            P.wait("t", "vv", fev[id(ps_f)])
            P.op("t", lambda t, m=m: t.transpose(
                ps_f[:, 0:NE], gT_sb[0:NE, m * 128:(m + 1) * 128], ident_f[0:NE, 0:NE]), inc="mm")
            P.wait("v", "mm", P.cnt["mm"])
            P.op("v", lambda v, m=m: v.tensor_copy(gtok[:, m * NE:(m + 1) * NE], ps_f[:, 0:NE]), inc="vv")
            fev[id(ps_f)] = P.cnt["vv"]
        if "gtok" in taps:
            P.wait("sy", "vv", P.cnt["vv"])
            P.dma("sy", tap_t["gtok"], gtok[:, :], "stp")

        for m in range(NT):
            gsl = gtok[:, m * NE:(m + 1) * NE]
            cur = gsl
            P.drain("v")
            for r in range(4):
                P.op("v", lambda v, cur=cur, r=r: v.tensor_reduce(mxs[:, r:r + 1], cur, axis=AX, op=ALU.max))
                P.drain("v")
                P.op("v", lambda v, cur=cur, r=r: v.tensor_scalar(
                    eqs[:, r * NE:(r + 1) * NE], cur, mxs[:, r:r + 1], None, ALU.is_equal))
                P.drain("v")
                if r < 3:
                    nxt = (gg1 if r % 2 == 0 else gg2)[:, :]
                    P.op("v", lambda v, cur=cur, r=r, nxt=nxt: v.scalar_tensor_tensor(
                        nxt, eqs[:, r * NE:(r + 1) * NE], -1e30, cur, ALU.mult, ALU.add))
                    P.drain("v")
                    cur = nxt
            for r in range(3):
                P.op("v", lambda v, r=r: v.tensor_sub(dd[:, r:r + 1], mxs[:, r + 1:r + 2], mxs[:, 0:1]))
            P.drain("v")
            P.op("v", lambda v: v.engine_nop(), inc="vv")
            P.wait("s", "vv", P.cnt["vv"])
            P.op("s", lambda s: s.activation(ee[:, :], dd[:, :], AF.Exp), inc="ss")
            P.wait("v", "ss", P.cnt["ss"])
            P.op("v", lambda v: v.tensor_reduce(sca[:, :], ee[:, :], axis=AX, op=ALU.add))
            P.drain("v")
            P.op("v", lambda v: v.tensor_scalar(sca[:, :], sca[:, :], 1.0, None, ALU.add))
            P.drain("v")
            P.op("v", lambda v: v.reciprocal(scb[:, :], sca[:, :]))
            P.drain("v")
            P.op("v", lambda v: v.tensor_copy(acc4[:, :], eqs[:, 0:NE]))
            P.drain("v")
            for r in range(3):
                P.op("v", lambda v, r=r: v.scalar_tensor_tensor(
                    acc4[:, :], eqs[:, (r + 1) * NE:(r + 2) * NE], ee[:, r:r + 1], acc4[:, :],
                    ALU.mult, ALU.add))
                P.drain("v")
            P.op("v", lambda v, m=m: v.tensor_scalar(
                ctok[:, m * NE:(m + 1) * NE], acc4[:, :], scb[:, 0:1], INV_AL, ALU.mult, ALU.mult), inc="vv")
            P.drain("v")
        if "ctok" in taps:
            P.wait("sy", "vv", P.cnt["vv"])
            P.dma("sy", tap_t["ctok"], ctok[:, :], "stp")

        ct_vv = P.cnt["vv"]
        P.wait("t", "vv", ct_vv)
        for m in range(NT):
            P.wait("t", "vv", fev[id(ps_f)])
            P.op("t", lambda t, m=m: t.transpose(
                ps_f[0:NE, 0:128], ctok[:, m * NE:(m + 1) * NE], ident_f[:, :]), inc="mm")
            P.wait("v", "mm", P.cnt["mm"])
            P.op("v", lambda v, m=m: v.tensor_copy(
                cT_sb[0:NE, m * 128:(m + 1) * 128], ps_f[0:NE, 0:128]), inc="vv")
            fev[id(ps_f)] = P.cnt["vv"]
        P.drain("v")
        P.op("v", lambda v: v.tensor_copy(cTb[0:NE, :], cT_sb[0:NE, :]), inc="vv")
        cTb_vv = P.cnt["vv"]
        P.wait("sy", "vv", cTb_vv)
        P.dma("sy", cspb.ap()[:, :], cTb[0:NE, :], "stc")
        cspb_st = P.cnt["stc"]

        if "xnT" in taps:
            P.wait("sy", "vv", xnT_vv)
            P.dma("sy", tap_t["xnT"], xnT[:, :], "stp")
        if "tT" in taps:
            P.wait("sy", "vv", tT_vv)
            P.dma("sy", tap_t["tT"], tT[:, :], "stp")
        if "qT" in taps:
            P.wait("sy", "st0", kv_st0)
            P.wait("sy", "st1", kv_st1)
            P.dma("sy", tap_t["qT"], qtsp.ap()[:, :], "stp")
        if "kv" in taps:
            P.wait("sy", "st0", kv_st0)
            P.wait("sy", "st1", kv_st1)
            P.dma("sy", tap_t["kv"], kv_in.ap(), "stp")
        P.dma("sy", xsp.ap()[:, :], x_sb[:, :], "stx")
        P.wait("sy", "stx", P.cnt["stx"])
        P.wait("sy", "st0", P.cnt["st0"])
        P.wait("sy", "st1", P.cnt["st1"])
        P.wait("sy", "stp", P.cnt["stp"])
        P.flush("p0")
        p0.close()
        quiesce()

        if phases >= 2:
            pm = ExitStack()
            accT = pm.enter_context(nc.sbuf_tensor("accT", [128, 8 * TOK], F32))
            hwT = pm.enter_context(nc.sbuf_tensor("hwT", [128, 8 * TOK], BF16))
            pm1 = ExitStack()
            R = [pm1.enter_context(nc.sbuf_tensor(f"wr{i}", [128, 8 * D], BF16)) for i in range(2)]
            glu = pm1.enter_context(nc.sbuf_tensor("glu", [128, TOK], BF16))
            sil = pm1.enter_context(nc.sbuf_tensor("sil", [128, TOK], BF16))
            liz = pm1.enter_context(nc.sbuf_tensor("liz", [128, TOK], BF16))
            cbc = pm1.enter_context(nc.sbuf_tensor("cbc", [128, TOK], BF16))
            b1g_sb = pm1.enter_context(nc.sbuf_tensor("b1gs", [128, NE * 8], F32))
            b1l_sb = pm1.enter_context(nc.sbuf_tensor("b1ls", [128, NE * 8], F32))
            b2b = pm1.enter_context(nc.sbuf_tensor("b2b", [NE, D], BF16))
            b2f = pm1.enter_context(nc.sbuf_tensor("b2f", [NE, D], F32))
            mps = [pm.enter_context(nc.psum_tensor(f"mps{i}", [128, 512], F32)) for i in range(4)]
            mpf = [pm.enter_context(nc.psum_tensor(f"mpf{i}", [128, 128], F32)) for i in range(2)]

            P.dma("sy", b1g_sb[:, :], h["b1g"], "lmc")
            P.dma("sy", b1l_sb[:, :], h["b1l"], "lmc")
            P.dma("sy", b2f[0:NE, :], h["b2"], "lmc")
            moe_ld0 = P.cnt["lmc"]
            P.wait("v", "lmc", moe_ld0)
            P.op("v", lambda v: v.tensor_scalar(b2b[0:NE, :], b2f[0:NE, :], 1.702, None, ALU.mult), inc="vv")
            b2b_vv = P.cnt["vv"]

            ps_ev = {id(p): 0 for p in mps}
            P.wait("t", "vv", b2b_vv)
            P.wait("t", "vv", cTb_vv)
            for ht in range(8):
                for tc in range(4):
                    ps = mps[(ht * 4 + tc) % 4]
                    P.wait("t", "vv", ps_ev[id(ps)])
                    P.op("t", lambda t, ht=ht, tc=tc, ps=ps: t.matmul(
                        ps[:, :], b2b[0:NE, ht * 128:(ht + 1) * 128],
                        cTb[0:NE, tc * 512:(tc + 1) * 512], start=True, stop=True), inc="mm")
                    P.wait("v", "mm", P.cnt["mm"])
                    P.op("v", lambda v, ht=ht, tc=tc, ps=ps: v.tensor_copy(
                        accT[:, ht * TOK + tc * 512: ht * TOK + (tc + 1) * 512], ps[:, :]), inc="vv")
                    ps_ev[id(ps)] = P.cnt["vv"]

            def wsrc(s):
                e, j = divmod(s, 3)
                return [h["w1g"], h["w1l"], h["w2T"]][j][e]

            def wbuf(e, j):
                return R[e % 2] if j in (0, 2) else R[1 - e % 2]

            def wsem(e, j):
                return "lr0" if (e % 2 == 0) == (j in (0, 2)) else "lr1"

            lg_after = {}

            def load_mat(e, j, gate_mm):
                if e >= NE:
                    return
                sem = wsem(e, j)
                P.wait("g", "mm", gate_mm)
                buf = wbuf(e, j)
                for dt in range(8):
                    P.dma("g", buf[:, dt * D:(dt + 1) * D], wsrc(3 * e + j)[dt], sem)
                lg_after[(e, j)] = (sem, P.cnt[sem])

            load_mat(0, 0, 0)
            load_mat(0, 1, 0)

            prev_w2_mm = 0
            prev_hw_vv = 0
            prev_cbc_vv = 0
            for e in range(NE):
                P.wait("sy", "stc", cspb_st)
                P.wait("sy", "vv", prev_cbc_vv)
                cb_ld = P.dma("sy", cbc[:, :],
                              cspb.ap()[e:e + 1, :].partition_broadcast(128)[:, 0, :], "lcb")
                P.wait("t", *lg_after[(e, 0)])
                P.wait("v", "mm", prev_w2_mm)
                P.drain("v")
                sil_ss = 0
                hw_vv = prev_hw_vv
                for ft in range(8):
                    for tc in range(4):
                        ps = mps[tc]
                        P.wait("t", "vv", ps_ev[id(ps)])
                        for dt in range(8):
                            P.op("t", lambda t, ft=ft, tc=tc, dt=dt, ps=ps, e=e: t.matmul(
                                ps[:, :], wbuf(e, 0)[:, dt * D + ft * 128: dt * D + (ft + 1) * 128],
                                tT[:, dt * TOK + tc * 512: dt * TOK + (tc + 1) * 512],
                                start=(dt == 0), stop=(dt == 7)), inc="mm" if dt == 7 else None)
                        P.wait("v", "mm", P.cnt["mm"])
                        P.wait("v", "ss", sil_ss)
                        P.op("v", lambda v, ft=ft, tc=tc, ps=ps, e=e: v.tensor_scalar(
                            glu[:, tc * 512:(tc + 1) * 512], ps[:, :],
                            b1g_sb[:, e * 8 + ft: e * 8 + ft + 1], 7.0, ALU.add, ALU.min), inc="vv")
                        ps_ev[id(ps)] = P.cnt["vv"]
                    glu_vv = P.cnt["vv"]
                    P.wait("s", "vv", glu_vv)
                    P.wait("s", "vv", hw_vv)
                    P.op("s", lambda s: s.activation(sil[:, :], glu[:, :], AF.Silu, scale=1.702), inc="ss")
                    sil_ss = P.cnt["ss"]
                    P.wait("t", *lg_after[(e, 1)])
                    for tc in range(4):
                        ps = mps[tc]
                        P.wait("t", "vv", ps_ev[id(ps)])
                        for dt in range(8):
                            P.op("t", lambda t, ft=ft, tc=tc, dt=dt, ps=ps, e=e: t.matmul(
                                ps[:, :], wbuf(e, 1)[:, dt * D + ft * 128: dt * D + (ft + 1) * 128],
                                tT[:, dt * TOK + tc * 512: dt * TOK + (tc + 1) * 512],
                                start=(dt == 0), stop=(dt == 7)), inc="mm" if dt == 7 else None)
                        P.wait("v", "mm", P.cnt["mm"])
                        P.op("v", lambda v, ft=ft, tc=tc, ps=ps, e=e: v.tensor_scalar(
                            liz[:, tc * 512:(tc + 1) * 512], ps[:, :],
                            b1l_sb[:, e * 8 + ft: e * 8 + ft + 1], 7.0, ALU.add, ALU.min), inc="vv")
                        ps_ev[id(ps)] = P.cnt["vv"]
                    P.drain("v")
                    P.op("v", lambda v: v.tensor_scalar(liz[:, :], liz[:, :], -7.0, 1.0, ALU.max, ALU.add))
                    P.drain("v")
                    P.wait("v", "lcb", cb_ld)
                    P.op("v", lambda v: v.tensor_mul(liz[:, :], liz[:, :], cbc[:, :]), inc="vv")
                    prev_cbc_vv = P.cnt["vv"]
                    P.drain("v")
                    P.wait("v", "ss", sil_ss)
                    P.op("v", lambda v, ft=ft: v.tensor_mul(
                        hwT[:, ft * TOK:(ft + 1) * TOK], sil[:, :], liz[:, :]), inc="vv")
                    hw_vv = P.cnt["vv"]
                    P.drain("v")
                prev_hw_vv = hw_vv
                ft_end_mm = P.cnt["mm"]
                load_mat(e, 2, ft_end_mm)
                load_mat(e + 1, 0, ft_end_mm)
                P.wait("t", *lg_after[(e, 2)])
                P.wait("t", "vv", hw_vv)
                for ht in range(8):
                    for tc in range(4):
                        ps = mps[tc]
                        P.wait("t", "vv", ps_ev[id(ps)])
                        for ft in range(8):
                            P.op("t", lambda t, ht=ht, tc=tc, ft=ft, ps=ps, e=e: t.matmul(
                                ps[:, :], wbuf(e, 2)[:, ft * D + ht * 128: ft * D + (ht + 1) * 128],
                                hwT[:, ft * TOK + tc * 512: ft * TOK + (tc + 1) * 512],
                                start=(ft == 0), stop=(ft == 7)), inc="mm" if ft == 7 else None)
                        P.wait("v", "mm", P.cnt["mm"])
                        P.drain("v")
                        P.op("v", lambda v, ht=ht, tc=tc, ps=ps: v.tensor_add(
                            accT[:, ht * TOK + tc * 512: ht * TOK + (tc + 1) * 512],
                            accT[:, ht * TOK + tc * 512: ht * TOK + (tc + 1) * 512], ps[:, :]), inc="vv")
                        ps_ev[id(ps)] = P.cnt["vv"]
                prev_w2_mm = P.cnt["mm"]
                load_mat(e + 1, 1, prev_w2_mm)

            if "accT" in taps:
                P.wait("sy", "vv", P.cnt["vv"])
                P.dma("sy", tap_t["accT"], accT[:, :], "stp")
            acc_vv = P.cnt["vv"]
            pm1.close()
            mo_t = pm.enter_context(nc.sbuf_tensor("mot", [128, 2 * D], F32))
            fev2 = {id(p): 0 for p in mpf}
            mo_st = {0: 0, 1: 0}
            P.wait("t", "vv", acc_vv)
            for m in range(NT):
                slot = m % 2
                msem = "mo0" if slot == 0 else "mo1"
                P.wait("v", msem, mo_st[slot])
                for dt in range(8):
                    ps = mpf[dt % 2]
                    P.wait("t", "vv", fev2[id(ps)])
                    P.op("t", lambda t, m=m, dt=dt, ps=ps: t.transpose(
                        ps[:, :], accT[:, dt * TOK + m * 128: dt * TOK + (m + 1) * 128],
                        ident_f[:, :]), inc="mm")
                    P.wait("v", "mm", P.cnt["mm"])
                    P.op("v", lambda v, dt=dt, ps=ps, slot=slot: v.tensor_copy(
                        mo_t[:, slot * D + dt * 128: slot * D + (dt + 1) * 128], ps[:, :]), inc="vv")
                    fev2[id(ps)] = P.cnt["vv"]
                P.wait("sy", "vv", P.cnt["vv"])
                P.dma("sy", moe_hbm.ap()[m], mo_t[:, slot * D:(slot + 1) * D], msem)
                mo_st[slot] = P.cnt[msem]
            if "moe" in taps:
                P.wait("sy", "mo0", P.cnt["mo0"])
                P.wait("sy", "mo1", P.cnt["mo1"])
                P.dma("sy", tap_t["moe"], moe_hbm.ap(), "stp")
            P.wait("sy", "mo0", P.cnt["mo0"])
            P.wait("sy", "mo1", P.cnt["mo1"])
            P.wait("sy", "stp", P.cnt["stp"])
            P.flush("moe")
            pm.close()
        pAB.close()
        quiesce()

        pb = ExitStack()
        x_sb = pb.enter_context(nc.sbuf_tensor("x1s", [128, NT * D], F32))
        P.wait("sy", "stx", P.cnt["stx"])
        x_rld = P.dma("sy", x_sb[:, :], xsp.ap()[:, :], "lxr")
        P.wait("v", "lxr", x_rld)

        if phases >= 3:
            pm2 = ExitStack()
            mo_t2 = pm2.enter_context(nc.sbuf_tensor("mot2", [128, 2 * D], F32))
            mo_vv = {0: 0, 1: 0}
            for m in range(NT):
                slot = m % 2
                msem = "mb0" if slot == 0 else "mb1"
                P.wait("sy", "vv", mo_vv[slot])
                ld = P.dma("sy", mo_t2[:, slot * D:(slot + 1) * D], moe_hbm.ap()[m], msem)
                P.wait("v", msem, ld)
                P.drain("v")
                P.op("v", lambda v, m=m, slot=slot: v.tensor_add(
                    x_sb[:, m * D:(m + 1) * D], x_sb[:, m * D:(m + 1) * D],
                    mo_t2[:, slot * D:(slot + 1) * D]), inc="vv")
                mo_vv[slot] = P.cnt["vv"]
            P.flush("moeadd")
            pm2.close()
            quiesce()

            pa = ExitStack()
            qT = pa.enter_context(nc.sbuf_tensor("qT1", [128, 8 * TOK], BF16))
            KT2 = pa.enter_context(nc.sbuf_tensor("KT2", [128, 8192], BF16))
            Vp = pa.enter_context(nc.sbuf_tensor("Vp", [128, 2 * 4160], BF16))
            VT2 = pa.enter_context(nc.sbuf_tensor("VT2", [128, 8192], BF16))
            Rel = pa.enter_context(nc.sbuf_tensor("Rel", [128, 2 * RELU], BF16))
            PT = pa.enter_context(nc.sbuf_tensor("PTt", [128, 2 * 1024], BF16))
            avT = pa.enter_context(nc.sbuf_tensor("avT", [128, 1024], F32))
            chb = pa.enter_context(nc.sbuf_tensor("chb", [128, 1], F32))
            bmx = pa.enter_context(nc.sbuf_tensor("bmx", [128, 1], F32))
            qb_s = pa.enter_context(nc.sbuf_tensor("qbs", [128, 1], F32))
            kb_s = pa.enter_context(nc.sbuf_tensor("kbs", [128, 1], F32))
            nm_s = pa.enter_context(nc.sbuf_tensor("nms", [128, 1], F32))
            nm_bc = pa.enter_context(nc.sbuf_tensor("nmbc", [128, 1], F32))
            sqt = pa.enter_context(nc.sbuf_tensor("sqt", [128, 512], BF16))

            psS = [pa.enter_context(nc.psum_tensor(f"psS{i}", [128, 1024], F32)) for i in range(2)]
            psAV = pa.enter_context(nc.psum_tensor("psAV", [128, 1024], F32))
            psN = [pa.enter_context(nc.psum_tensor("psN0", [128, 512], F32))]
            psT = pa.enter_context(nc.psum_tensor("psTb", [128, 128], BF16))

            q_rld = P.dma("sy", qT[:, :], qtsp.ap()[:, :], "lqr")
            P.op("v", lambda v: v.memset(Vp[:, :], 1.0), inc="vv")
            vp_ones = P.cnt["vv"]
            P.wait("t", "lqr", q_rld)
            P.wait("v", "lqr", q_rld)
            P.wait("g", "cc", 1)
            P.op("g", lambda g: g.engine_nop(), inc="sg")
            ag_sg = P.cnt["sg"]
            P.wait("sy", "sg", ag_sg)

            last_exp_ss = 0
            last_av_mm = 0
            last_rel_use_vv = 0
            last_ch_vv = 0
            last_vp_read_vv = 0
            vp_tr_vv = {0: 0}
            vp_vv = 0
            vp_tr_mm = 0
            last_kt_use_vv = 0
            last_nm_read_ss = 0
            last_bmx_vv = 0
            last_q_use_vv = 0
            psN_mm = {id(p): 0 for p in psN}
            psN_vv = {id(p): 0 for p in psN}
            psAV_ev = 0
            avT_mm = 0
            for pt in range(8):
                P.wait("g", "vv", last_rel_use_vv)
                rel_lg = 0
                for hh in range(2):
                    rel_lg = P.dma("g", Rel[:, hh * RELU:(hh + 1) * RELU], h["relT"][2 * pt + hh], "lrel")
                for hh in range(2):
                    P.wait("s", "vv", last_ch_vv)
                    chld = P.dma("s", chb[:, :],
                                 h["chv"][2 * pt + hh:2 * pt + hh + 1, :]
                                 .partition_broadcast(128)[:, 0, :], "lch")
                    P.wait("v", "lch", chld)
                    P.wait("v", "lrel", rel_lg)
                    P.op("v", lambda v, hh=hh: v.tensor_scalar(
                        Rel[:, hh * RELU:(hh + 1) * RELU], Rel[:, hh * RELU:(hh + 1) * RELU],
                        chb[:, 0:1], None, ALU.subtract), inc="vv")
                    P.drain("v")
                    last_ch_vv = P.cnt["vv"]
                rel_vv = P.cnt["vv"]
                P.op("v", lambda v: v.tensor_reduce(bmx[:, :], Rel[:, :], axis=AX, op=ALU.max), inc="vv")
                P.wait("t", "vv", P.cnt["vv"])
                ps0 = psN[0]
                P.wait("t", "mm", psN_mm[id(ps0)])
                P.wait("t", "vv", psN_vv[id(ps0)])
                P.op("t", lambda t, ps0=ps0: t.transpose(
                    ps0[0:1, 0:128], bmx[:, :], ident_f[:, :]), inc="mm")
                psN_mm[id(ps0)] = P.cnt["mm"]
                P.wait("v", "mm", P.cnt["mm"])
                P.op("v", lambda v, ps0=ps0: v.tensor_reduce(
                    bmx[0:1, :], ps0[0:1, 0:128], axis=AX, op=ALU.max), inc="vv")
                psN_vv[id(ps0)] = P.cnt["vv"]
                P.drain("v")
                P.op("v", lambda v: v.tensor_scalar(bmx[0:1, :], bmx[0:1, :], 0.0, None, ALU.max), inc="vv")
                last_bmx_vv = P.cnt["vv"]

                for b in range(B):
                    P.wait("sy", "mm", last_av_mm)
                    P.wait("sy", "mm", vp_tr_mm)
                    P.wait("sy", "vv", last_kt_use_vv)
                    kt_ld = 0
                    for r in range(NCORES):
                        kt_ld = P.dma("sy", KT2[:, r * 1024:(r + 1) * 1024],
                                      kv_out.ap()[r, 0, pt * 128:(pt + 1) * 128,
                                                  b * 1024:(b + 1) * 1024], "lkt")
                    P.wait("sy", "vv", last_vp_read_vv)
                    for r in range(NCORES):
                        P.dma("sy", VT2[:, r * 1024:(r + 1) * 1024],
                              kv_out.ap()[r, 1, pt * 128:(pt + 1) * 128,
                                          b * 1024:(b + 1) * 1024], "lvt")
                    vt_ld = P.cnt["lvt"]
                    P.wait("t", "lvt", vt_ld)
                    P.wait("t", "mm", last_av_mm)
                    P.wait("v", "mm", last_av_mm)
                    P.wait("v", "vv", vp_ones)
                    psT_mm = 0
                    for hh in range(2):
                        for jpos in range(64):
                            P.wait("t", "vv", P.cnt["vv"] if False else vp_tr_vv.get(0, 0))
                            P.op("t", lambda t, hh=hh, jpos=jpos: t.transpose(
                                psT[:, 0:64], VT2[hh * 64:(hh + 1) * 64, jpos * 128:(jpos + 1) * 128],
                                ident_b[hh * 64:(hh + 1) * 64, hh * 64:(hh + 1) * 64]), inc="mm")
                            P.wait("v", "mm", P.cnt["mm"])
                            P.op("v", lambda v, hh=hh, jpos=jpos: v.tensor_copy(
                                Vp[:, hh * 4160 + jpos * 65: hh * 4160 + jpos * 65 + 64],
                                psT[:, 0:64]), inc="vv")
                            vp_tr_vv[0] = P.cnt["vv"]
                    vp_vv = P.cnt["vv"]
                    vp_tr_mm = P.cnt["mm"]

                    if pt == 0 and b == 0 and "dbg" in taps:
                        P.wait("sy", "lkt", kt_ld)
                        P.wait("sy", "lvt", vt_ld)
                        P.wait("sy", "lqr", q_rld)
                        P.dma("sy", tap_t["dbg"][:, 0:8320], Vp[:, :], "stp")
                        P.dma("sy", tap_t["dbg"][:, 8448:16640], KT2[:, :], "stp")
                        P.dma("sy", tap_t["dbg"][:, 16640:18688], qT[:, 0:2048], "stp")
                        P.wait("sy", "stp", P.cnt["stp"])
                    for hh in range(2):
                        P.wait("v", "lkt", kt_ld)
                        for cn in range(18):
                            if cn < 16:
                                src = KT2[hh * 64:(hh + 1) * 64, cn * 512:(cn + 1) * 512]
                                first, dstb = (cn == 0), kb_s
                            else:
                                base = pt * TOK + b * 1024 + (cn - 16) * 512
                                src = qT[hh * 64:(hh + 1) * 64, base:base + 512]
                                first, dstb = (cn == 16), qb_s
                            ps = psN[0]
                            P.wait("v", "mm", psN_mm[id(ps)])
                            P.op("v", lambda v, src=src, hh=hh: v.tensor_mul(
                                sqt[hh * 64:(hh + 1) * 64, 0:512], src, src), inc="vv")
                            P.wait("t", "vv", P.cnt["vv"])
                            P.wait("t", "vv", psN_vv[id(ps)])
                            P.op("t", lambda t, hh=hh, ps=ps: t.matmul(
                                ps[0:1, :], ones_b[hh * 64:(hh + 1) * 64, 0:1],
                                sqt[hh * 64:(hh + 1) * 64, 0:512], start=True, stop=True), inc="mm")
                            psN_mm[id(ps)] = P.cnt["mm"]
                            P.wait("v", "mm", P.cnt["mm"])
                            P.op("v", lambda v, ps=ps: v.tensor_reduce(
                                sca[0:1, :], ps[0:1, :], axis=AX, op=ALU.max), inc="vv")
                            psN_vv[id(ps)] = P.cnt["vv"]
                            P.drain("v")
                            if first:
                                P.op("v", lambda v, dstb=dstb: v.tensor_copy(dstb[0:1, :], sca[0:1, :]))
                            else:
                                P.op("v", lambda v, dstb=dstb: v.tensor_max(dstb[0:1, :], dstb[0:1, :], sca[0:1, :]))
                            P.drain("v")
                        last_q_use_vv = P.cnt["vv"]
                        P.op("v", lambda v: v.tensor_add(nm_s[0:1, :], qb_s[0:1, :], kb_s[0:1, :]))
                        P.drain("v")
                        P.op("v", lambda v: v.scalar_tensor_tensor(
                            nm_s[0:1, :], nm_s[0:1, :], 0.5, bmx[0:1, :], ALU.mult, ALU.add))
                        P.drain("v")
                        P.wait("v", "snm", P.cnt["snm"])
                        P.op("v", lambda v: v.tensor_scalar(
                            nm_s[0:1, :], nm_s[0:1, :], -1.0, None, ALU.mult), inc="vv")
                        P.wait("sy", "vv", P.cnt["vv"])
                        P.wait("sy", "lnm", P.cnt["lnm"])
                        P.dma("sy", nmsp.ap(), nm_s[0:1, 0:1], "snm")
                        P.wait("sy", "snm", P.cnt["snm"])
                        P.wait("sy", "ss", last_nm_read_ss)
                        nm_ld = P.dma("sy", nm_bc[:, :],
                                      nmsp.ap().partition_broadcast(128)[:, 0, :], "lnm")

                        P.wait("s", "lnm", nm_ld)
                        P.wait("s", "mm", last_av_mm)
                        P.wait("t", "vv", vp_vv)
                        P.wait("t", "lkt", kt_ld)
                        P.wait("t", "vv", rel_vv)
                        P.wait("t", "ss", last_exp_ss)
                        P.wait("t", "vv", psAV_ev)
                        exp_ss = {}
                        av_mm = {}
                        for j in range(64):
                            ps = psS[j % 2]
                            if j >= 2:
                                P.wait("t", "ss", exp_ss[j - 2])
                            for ic in range(2):
                                P.op("t", lambda t, j=j, ic=ic, hh=hh, pt=pt, b=b, ps=ps: t.matmul(
                                    ps[:, ic * 512:(ic + 1) * 512],
                                    KT2[hh * 64:(hh + 1) * 64, j * 128:(j + 1) * 128],
                                    qT[hh * 64:(hh + 1) * 64,
                                       pt * TOK + b * 1024 + ic * 512: pt * TOK + b * 1024 + (ic + 1) * 512],
                                    start=True, stop=True), inc="mm")
                            qk_mm = P.cnt["mm"]
                            nbias = 0
                            for (k, m0) in segs_by_j.get(j, []):
                                P.wait("v", "mm", qk_mm)
                                P.op("v", lambda v, k=k, m0=m0, hh=hh, ps=ps: v.tensor_add(
                                    ps[:, k * 128:(k + 1) * 128], ps[:, k * 128:(k + 1) * 128],
                                    Rel[:, hh * RELU + m0: hh * RELU + m0 + 128]), inc="vv")
                                nbias += 1
                            P.wait("s", "mm", qk_mm)
                            if nbias:
                                P.wait("s", "vv", P.cnt["vv"])
                            if j >= 2:
                                P.wait("s", "mm", av_mm[j - 2])
                            P.op("s", lambda s, j=j, ps=ps: s.activation(
                                PT[:, (j % 2) * 1024:(j % 2 + 1) * 1024], ps[:, :], AF.Exp,
                                bias=nm_bc[:, 0:1]), inc="ss")
                            exp_ss[j] = P.cnt["ss"]
                            if j == 0 and pt == 0 and b == 0 and hh == 0 and "dbg2" in taps:
                                P.wait("sy", "ss", exp_ss[j])
                                P.dma("sy", tap_t["dbg2"][:, 0:512], PT.bitcast(F32)[:, 0:512], "stp")
                                P.dma("sy", tap_t["dbg2"][:, 1024:1025], nm_bc[:, :], "stp", allow_slow_non_contiguous=True)
                                P.dma("sy", tap_t["dbg2"][:, 1025:1026], bmx[:, :], "stp", allow_slow_non_contiguous=True)
                                P.wait("sy", "stp", P.cnt["stp"])
                                P.wait("s", "stp", P.cnt["stp"])
                            P.wait("t", "ss", exp_ss[j])
                            for ic in range(2):
                                P.op("t", lambda t, j=j, ic=ic, hh=hh: t.matmul(
                                    psAV[0:65, ic * 512:(ic + 1) * 512],
                                    Vp[:, hh * 4160 + j * 65: hh * 4160 + j * 65 + 65],
                                    PT[:, (j % 2) * 1024 + ic * 512: (j % 2) * 1024 + (ic + 1) * 512],
                                    start=(j == 0), stop=(j == 63)), inc="mm")
                            av_mm[j] = P.cnt["mm"]
                        last_exp_ss = exp_ss[63]
                        last_av_mm = av_mm[63]
                        last_nm_read_ss = exp_ss[63]
                        last_rel_use_vv = P.cnt["vv"]
                        last_kt_use_vv = max(last_q_use_vv, P.cnt["vv"])
                        last_vp_read_vv = P.cnt["vv"]

                        P.wait("v", "mm", av_mm[63])
                        P.wait("v", "mm", avT_mm)
                        P.op("v", lambda v: v.tensor_copy(avT[0:65, :], psAV[0:65, :]), inc="vv")
                        psAV_ev = P.cnt["vv"]
                        if pt == 0 and b == 0 and hh == 0 and "dbg3" in taps:
                            P.wait("sy", "vv", psAV_ev)
                            P.dma("sy", tap_t["dbg3"][0:65, :], avT[0:65, :], "stp")
                            P.wait("sy", "stp", P.cnt["stp"])
                            P.wait("v", "stp", P.cnt["stp"])
                        P.wait("t", "vv", psAV_ev)
                        for k in range(8):
                            ps = psN[0]
                            P.wait("t", "mm", psN_mm[id(ps)])
                            P.wait("t", "vv", psN_vv[id(ps)])
                            P.op("t", lambda t, k=k, ps=ps: t.transpose(
                                ps[:, 0:65], avT[0:65, k * 128:(k + 1) * 128], ident_f[0:65, 0:65]), inc="mm")
                            psN_mm[id(ps)] = P.cnt["mm"]
                            P.wait("v", "mm", P.cnt["mm"])
                            mcol = (b * 8 + k) * D + (2 * pt + hh) * 64
                            P.op("v", lambda v, ps=ps: v.reciprocal(sca[:, :], ps[:, 64:65]))
                            P.drain("v")
                            P.op("v", lambda v, ps=ps, mcol=mcol: v.scalar_tensor_tensor(
                                x_sb[:, mcol:mcol + 64], ps[:, 0:64], sca[:, 0:1],
                                x_sb[:, mcol:mcol + 64], ALU.mult, ALU.add), inc="vv")
                            psN_vv[id(ps)] = P.cnt["vv"]
                            P.drain("v")
                        avT_mm = P.cnt["mm"]

            if "x1" in taps:
                P.wait("sy", "vv", P.cnt["vv"])
                for m in range(NT):
                    P.dma("sy", tap_t["x1"][m], x_sb[:, m * D:(m + 1) * D], "stp")
                P.wait("sy", "stp", P.cnt["stp"])
            P.flush("attn")
            pa.close()
            quiesce()

        if phases >= 4:
            wsbc = pb.enter_context(nc.sbuf_tensor("wsbc", [128, D], F32))
            fob = pb.enter_context(nc.sbuf_tensor("fob", [128, D], F32))
            tmpc = pb.enter_context(nc.sbuf_tensor("tmpc", [128, D], F32))
            tmpe = pb.enter_context(nc.sbuf_tensor("tmpe", [128, D], F32))
            tmpd = pb.enter_context(nc.sbuf_tensor("tmpd", [128, D], BF16))
            tmpd2 = pb.enter_context(nc.sbuf_tensor("tmpd2", [128, D], BF16))
            ws_sb = pb.enter_context(nc.sbuf_tensor("wss", [128, 8], F32))
            ars = pb.enter_context(nc.sbuf_tensor("ars", [128, 520], F32))
            ctb = pb.enter_context(nc.sbuf_tensor("ctb", [128, 520], BF16))
            lps = [pb.enter_context(nc.psum_tensor(f"lps{i}", [128, 512], F32)) for i in range(2)]
            lpo = [pb.enter_context(nc.psum_tensor(f"lpo{i}", [128, 256], F32)) for i in range(2)]
            lpt = [pb.enter_context(nc.psum_tensor(f"lpt{i}", [128, 128], BF16)) for i in range(2)]
            lpf = pb.enter_context(nc.psum_tensor("lpf", [128, 512], F32))

            pla = ExitStack()
            wf = pla.enter_context(nc.sbuf_tensor("wf", [128, 8 * D], F32))
            wqb = pla.enter_context(nc.sbuf_tensor("wqb", [128, 8 * D], BF16))
            rw_sb = pla.enter_context(nc.sbuf_tensor("rws", [128, 8], F32))
            for dt in range(8):
                P.dma("sy", wf[:, dt * D:(dt + 1) * D], h["wf1"][dt], "lwf")
            P.dma("sy", fob[:, :], h["bo1"].partition_broadcast(128)[:, 0, :], "lfb")
            P.dma("sy", tmpc[:, :], h["bf1"].partition_broadcast(128)[:, 0, :], "lfb")
            P.wait("v", "lwf", P.cnt["lwf"])
            P.wait("v", "lfb", P.cnt["lfb"])
            P.drain("v")
            P.op("v", lambda v: v.tensor_add(fob[:, :], fob[:, :], tmpc[:, :]), inc="vv")
            P.drain("v")
            for dt in range(8):
                wt = wf[:, dt * D:(dt + 1) * D]
                P.op("v", lambda v, wt=wt, dt=dt: v.tensor_reduce(
                    ws_sb[:, dt:dt + 1], wt, axis=AX, op=ALU.max, apply_absolute_value=True))
                P.drain("v")
                P.op("v", lambda v, dt=dt: v.tensor_scalar(
                    ws_sb[:, dt:dt + 1], ws_sb[:, dt:dt + 1], 1e-8, None, ALU.max))
                P.drain("v")
                P.op("v", lambda v, dt=dt: v.reciprocal(sca[:, :], ws_sb[:, dt:dt + 1]))
                P.drain("v")
                P.op("v", lambda v, dt=dt: v.tensor_scalar(rw_sb[:, dt:dt + 1], sca[:, :], 127.0, None, ALU.mult))
                P.drain("v")
                P.op("v", lambda v, dt=dt: v.tensor_scalar(
                    ws_sb[:, dt:dt + 1], ws_sb[:, dt:dt + 1], 1.0 / 127.0, None, ALU.mult))
                P.drain("v")
                P.op("v", lambda v, wt=wt, dt=dt: v.tensor_scalar(
                    tmpc[:, :], wt, rw_sb[:, dt:dt + 1], RC, ALU.mult, ALU.add))
                P.drain("v")
                P.op("v", lambda v: v.tensor_scalar(tmpc[:, :], tmpc[:, :], RC, 127.0, ALU.subtract, ALU.min))
                P.drain("v")
                P.op("v", lambda v, dt=dt: v.tensor_scalar(
                    wqb[:, dt * D:(dt + 1) * D], tmpc[:, :], -128.0, None, ALU.max), inc="vv")
                P.drain("v")
            wq_vv = P.cnt["vv"]
            pst3 = {id(p): 0 for p in lpt}
            P.wait("t", "vv", wq_vv)
            wqt_st = {0: 0, 1: 0}
            for dt in range(8):
                for ot in range(8):
                    ps = lpt[(dt * 8 + ot) % 2]
                    P.wait("t", "vv", pst3[id(ps)])
                    P.op("t", lambda t, dt=dt, ot=ot, ps=ps: t.transpose(
                        ps[:, :], wqb[:, dt * D + ot * 128: dt * D + (ot + 1) * 128], ident_b[:, :]), inc="mm")
                    P.wait("v", "mm", P.cnt["mm"])
                    slot = (dt * 8 + ot) % 2
                    wsem = "swq0" if slot == 0 else "swq1"
                    P.wait("v", wsem, wqt_st[slot])
                    P.op("v", lambda v, ps=ps, slot=slot: v.tensor_copy(
                        tmpd[:, slot * 128:(slot + 1) * 128] if False else tmpd[:, slot * 128:(slot + 1) * 128],
                        ps[:, :]), inc="vv")
                    pst3[id(ps)] = P.cnt["vv"]
                    P.wait("sy", "vv", P.cnt["vv"])
                    P.dma("sy", wqt_sp.ap()[:, ot * D + dt * 128: ot * D + (dt + 1) * 128],
                          tmpd[:, slot * 128:(slot + 1) * 128], wsem)
                    wqt_st[slot] = P.cnt[wsem]
            ws_ld = row_broadcast(ws_sb[:, :], 8, wsbc, lpf, tmpc)
            P.wait("sy", "swq0", P.cnt["swq0"])
            P.wait("sy", "swq1", P.cnt["swq1"])
            P.flush("l1a")
            pla.close()
            quiesce()

            plb = ExitStack()
            SKIP = type("S", (), {})()
            qsT = plb.enter_context(nc.sbuf_tensor("qsT", [128, 4 * TOK], BF16))
            lb12 = ExitStack()
            xn2T = lb12.enter_context(nc.sbuf_tensor("xn2T", [128, 8 * TOK], BF16))
            ek_sb = lb12.enter_context(nc.sbuf_tensor("eks", [128, NT * 512], BF16))
            v1p = lb12.enter_context(nc.sbuf_tensor("v1p", [128, NT * 520], BF16))
            pw = lb12.enter_context(nc.sbuf_tensor("pw", [128, 8 * 512], BF16))
            xq_st = 0
            pst4 = {id(p): 0 for p in lpt}
            last_tr2 = 0
            for m in (range(NT) if l1sub >= 2 else []):
                xs = x_sb[:, m * D:(m + 1) * D]
                ln_tile(xs, m, tmpc, do_rho=False)
                P.op("v", lambda v, xs=xs, m=m: v.tensor_scalar(
                    tmpc[:, :], xs, mu_sb[:, m:m + 1], rstd_sb[:, m:m + 1], ALU.subtract, ALU.mult))
                P.drain("v")
                P.op("v", lambda v: v.tensor_reduce(
                    sca[:, :], tmpc[:, :], axis=AX, op=ALU.max, apply_absolute_value=True))
                P.drain("v")
                P.op("v", lambda v: v.tensor_scalar(sca[:, :], sca[:, :], 1e-8, None, ALU.max))
                P.drain("v")
                P.op("v", lambda v, m=m: v.tensor_scalar(
                    xsc_sb[:, m:m + 1], sca[:, :], 1.0 / 127.0, None, ALU.mult))
                P.op("v", lambda v: v.reciprocal(scb[:, :], sca[:, :]))
                P.drain("v")
                P.op("v", lambda v: v.tensor_scalar(scb[:, :], scb[:, :], 127.0, None, ALU.mult))
                P.drain("v")
                P.op("v", lambda v: v.tensor_scalar(tmpe[:, :], tmpc[:, :], scb[:, 0:1], RC, ALU.mult, ALU.add))
                P.drain("v")
                P.op("v", lambda v: v.tensor_scalar(tmpe[:, :], tmpe[:, :], RC, 127.0, ALU.subtract, ALU.min))
                P.drain("v")
                P.wait("v", "sxq", xq_st)
                P.op("v", lambda v: v.tensor_scalar(tmpd[:, :], tmpe[:, :], -128.0, None, ALU.max), inc="vv")
                P.drain("v")
                P.wait("sy", "vv", P.cnt["vv"])
                xq_st = P.dma("sy", xqsp.ap()[m], tmpd[:, :], "sxq")
                P.wait("v", "mm", last_tr2)
                P.op("v", lambda v: v.tensor_copy(tmpd2[:, :], tmpc[:, :]), inc="vv")
                P.wait("t", "vv", P.cnt["vv"])
                for dt in range(8):
                    ps = lpt[dt % 2]
                    P.wait("t", "vv", pst4[id(ps)])
                    P.op("t", lambda t, dt=dt, ps=ps: t.transpose(
                        ps[:, :], tmpd2[:, dt * 128:(dt + 1) * 128], ident_b[:, :]), inc="mm")
                    P.wait("v", "mm", P.cnt["mm"])
                    P.op("v", lambda v, dt=dt, m=m, ps=ps: v.tensor_copy(
                        xn2T[:, dt * TOK + m * 128: dt * TOK + (m + 1) * 128], ps[:, :]), inc="vv")
                    pst4[id(ps)] = P.cnt["vv"]
                last_tr2 = P.cnt["mm"]
            xn2T_vv = P.cnt["vv"]
            if "xn2T" in taps:
                P.wait("sy", "vv", xn2T_vv)
                P.dma("sy", tap_t["xn2T"], xn2T[:, :], "stp")

            P.op("v", lambda v: v.memset(v1p[:, :], 1.0), inc="vv")
            P.drain("v")
            lev = {id(p): 0 for p in lps}
            P.wait("t", "vv", xn2T_vv)
            pw_lg = 0
            for which in (["q", "k", "v"] if l1sub >= 3 else []):
                P.wait("g", "mm", P.cnt["mm"])
                for dt in range(8):
                    pw_lg = P.dma("g", pw[:, dt * 512:(dt + 1) * 512],
                                  h["w" + which + "1"][dt], "lpw")
                P.wait("t", "lpw", pw_lg)
                for m in range(NT):
                    ps = lps[m % 2]
                    P.wait("t", "vv", lev[id(ps)])
                    P.wait("t", "ss", P.cnt["ss"] if which == "k" else 0)
                    for dt in range(8):
                        P.op("t", lambda t, m=m, dt=dt, ps=ps: t.matmul(
                            ps[:, :], xn2T[:, dt * TOK + m * 128: dt * TOK + (m + 1) * 128],
                            pw[:, dt * 512:(dt + 1) * 512],
                            start=(dt == 0), stop=(dt == 7)), inc="mm" if dt == 7 else None)
                    mmv = P.cnt["mm"]
                    if which == "q":
                        P.wait("v", "mm", mmv)
                        P.op("v", lambda v, ps=ps: v.tensor_copy(tmpc[:, 0:512], ps[:, :]))
                        P.drain("v")
                        for hd in range(8):
                            w0 = hd * 64
                            P.op("v", lambda v, w0=w0: v.tensor_reduce(
                                sca[:, :], tmpc[:, w0:w0 + 64], axis=AX, op=ALU.max))
                            P.drain("v")
                            P.op("v", lambda v: v.tensor_scalar(scb[:, :], sca[:, :], -1.0, None, ALU.mult), inc="vv")
                            P.wait("s", "vv", P.cnt["vv"])
                            P.op("s", lambda s, w0=w0: s.activation(
                                tmpe[:, w0:w0 + 64], tmpc[:, w0:w0 + 64], AF.Exp, bias=scb[:, 0:1]), inc="ss")
                            P.wait("v", "ss", P.cnt["ss"])
                            P.op("v", lambda v, w0=w0: v.tensor_reduce(
                                sca[:, :], tmpe[:, w0:w0 + 64], axis=AX, op=ALU.add))
                            P.drain("v")
                            P.op("v", lambda v: v.reciprocal(scb[:, :], sca[:, :]))
                            P.drain("v")
                            P.op("v", lambda v, w0=w0: v.tensor_scalar(
                                tmpd2[:, w0:w0 + 64], tmpe[:, w0:w0 + 64], scb[:, 0:1], 0.125,
                                ALU.mult, ALU.mult), inc="vv")
                            P.drain("v")
                        lev[id(ps)] = P.cnt["vv"]
                        P.wait("t", "vv", P.cnt["vv"])
                        for ot in range(4):
                            pst = lpt[ot % 2]
                            P.wait("t", "vv", pst4[id(pst)])
                            P.op("t", lambda t, ot=ot, pst=pst: t.transpose(
                                pst[:, :], tmpd2[:, ot * 128:(ot + 1) * 128], ident_b[:, :]), inc="mm")
                            P.wait("v", "mm", P.cnt["mm"])
                            P.op("v", lambda v, ot=ot, m=m, pst=pst: v.tensor_copy(
                                qsT[:, ot * TOK + m * 128: ot * TOK + (m + 1) * 128], pst[:, :]), inc="vv")
                            pst4[id(pst)] = P.cnt["vv"]
                    elif which == "k":
                        P.wait("s", "mm", mmv)
                        P.op("s", lambda s, m=m, ps=ps: s.activation(
                            ek_sb[:, m * 512:(m + 1) * 512], ps[:, :], AF.Exp), inc="ss")
                        lev[id(ps)] = 0
                        P.wait("t", "ss", P.cnt["ss"])
                    else:
                        P.wait("v", "mm", mmv)
                        for hd in range(8):
                            P.op("v", lambda v, m=m, hd=hd, ps=ps: v.tensor_copy(
                                v1p[:, m * 520 + hd * 65: m * 520 + hd * 65 + 64],
                                ps[:, hd * 64:(hd + 1) * 64]), inc="vv")
                        lev[id(ps)] = P.cnt["vv"]
            ek_ss = P.cnt["ss"]
            v1_vv = P.cnt["vv"]
            qs_vv = P.cnt["vv"]
            if "ek" in taps:
                P.wait("sy", "ss", ek_ss)
                P.dma("sy", tap_t["ek"], ek_sb[:, :], "stp")
            if "qsT" in taps:
                P.wait("sy", "vv", qs_vv)
                P.dma("sy", tap_t["qsT"], qsT[:, :], "stp")

            DO_D = l1sub >= 4
            if DO_D:
                P.wait("t", "ss", ek_ss)
                P.wait("t", "vv", v1_vv)
                P.wait("t", "vv", lev[id(lps[0])])
                P.wait("t", "vv", lev[id(lps[1])])
                for b in range(B):
                    psC = lps[b]
                    for hd in range(8):
                        bp = (hd % 2) * 64
                        cb = hd // 2
                        for mk in range(8):
                            mm_ = b * 8 + mk
                            P.op("t", lambda t, b=b, hd=hd, mk=mk, bp=bp, cb=cb, mm_=mm_, psC=psC: t.matmul(
                                psC[bp:bp + 64, cb * 65: cb * 65 + 65],
                                ek_sb[:, mm_ * 512 + hd * 64: mm_ * 512 + (hd + 1) * 64],
                                v1p[:, mm_ * 520 + hd * 65: mm_ * 520 + (hd + 1) * 65],
                                start=(mk == 0), stop=(mk == 7),
                                tile_position=(0, bp)), inc="mm" if mk == 7 else None)
                ctxp_mm = P.cnt["mm"]
                P.wait("v", "mm", ctxp_mm)
                P.op("v", lambda v: v.tensor_copy(ars[:, 0:260], lps[0][:, 0:260]))
                P.drain("v")
                P.op("v", lambda v: v.tensor_copy(ars[:, 260:520], lps[1][:, 0:260]), inc="vv")
                lev[id(lps[0])] = P.cnt["vv"]
                lev[id(lps[1])] = P.cnt["vv"]
                P.wait("sy", "vv", P.cnt["vv"])
                P.dma("sy", ar_in.ap()[:, :], ars[:, :], "sar")
                P.wait("g", "sar", P.cnt["sar"])
                P.op("g", lambda g: g.collective_compute(
                    "AllReduce", ALU.add, replica_groups=[list(range(NCORES))],
                    ins=[ar_in.ap()], outs=[ar_out.ap()]), inc="cc")
            P.flush("l1d")
            lb12.close()
            quiesce()

            DO_E = l1sub >= 5
            if DO_E:
                xqT = plb.enter_context(nc.sbuf_tensor("xqT", [128, 8 * TOK], BF16))
                wqT1 = plb.enter_context(nc.sbuf_tensor("wqT1b", [128, 8 * D], BF16))
                P.wait("sy", "swq0", P.cnt["swq0"])
                P.wait("sy", "swq1", P.cnt["swq1"])
                wq_rld = P.dma("sy", wqT1[:, :], wqt_sp.ap()[:, :], "lwq")
                if "wqT1" in taps:
                    P.dma("sy", tap_t["wqT1"], wqt_sp.ap()[:, :], "stp")
                pst5 = {id(p): 0 for p in lpt}
                xq_rld = {0: 0, 1: 0}
                P.wait("sy", "sxq", P.cnt["sxq"])
                for m in range(NT):
                    slot = m % 2
                    qsem = "lxq0" if slot == 0 else "lxq1"
                    P.wait("sy", "mm", pst5.get(("rld", slot), 0))
                    ld = P.dma("sy", (tmpd if slot == 0 else tmpd2)[:, :], xqsp.ap()[m], qsem)
                    xq_rld[slot] = ld
                    P.wait("t", qsem, ld)
                    src = tmpd if slot == 0 else tmpd2
                    for dt in range(8):
                        ps = lpt[dt % 2]
                        P.wait("t", "vv", pst5[id(ps)])
                        P.op("t", lambda t, dt=dt, ps=ps, src=src: t.transpose(
                            ps[:, :], src[:, dt * 128:(dt + 1) * 128], ident_b[:, :]), inc="mm")
                        P.wait("v", "mm", P.cnt["mm"])
                        P.op("v", lambda v, dt=dt, m=m, ps=ps: v.tensor_copy(
                            xqT[:, dt * TOK + m * 128: dt * TOK + (m + 1) * 128], ps[:, :]), inc="vv")
                        pst5[id(ps)] = P.cnt["vv"]
                    pst5[("rld", slot)] = P.cnt["mm"]
                xqT_vv = P.cnt["vv"]
                P.wait("v", "lrow", ws_ld)
                for m in range(NT):
                    P.drain("v")
                    P.op("v", lambda v, m=m: v.tensor_add(
                        x_sb[:, m * D:(m + 1) * D], x_sb[:, m * D:(m + 1) * D], fob[:, :]), inc="vv")
                P.drain("v")
                P.wait("t", "vv", xqT_vv)
                P.wait("t", "lwq", wq_rld)
                for m in range(NT):
                    for half in range(2):
                        ps = lps[half]
                        P.wait("t", "vv", lev[id(ps)])
                        for dt in range(8):
                            P.op("t", lambda t, m=m, half=half, dt=dt, ps=ps: t.matmul(
                                ps[:, :], xqT[:, dt * TOK + m * 128: dt * TOK + (m + 1) * 128],
                                wqT1[:, dt * D + half * 512: dt * D + (half + 1) * 512],
                                start=(dt == 0), stop=(dt == 7)), inc="mm" if dt == 7 else None)
                        P.wait("v", "mm", P.cnt["mm"])
                        P.drain("v")
                        P.op("v", lambda v, m=m, half=half, ps=ps: v.scalar_tensor_tensor(
                            tmpc[:, 0:512], ps[:, :], xsc_sb[:, m:m + 1],
                            wsbc[:, half * 512:(half + 1) * 512], ALU.mult, ALU.mult))
                        P.drain("v")
                        P.op("v", lambda v, m=m, half=half: v.tensor_add(
                            x_sb[:, m * D + half * 512: m * D + (half + 1) * 512],
                            x_sb[:, m * D + half * 512: m * D + (half + 1) * 512],
                            tmpc[:, 0:512]), inc="vv")
                        lev[id(ps)] = P.cnt["vv"]

            DO_F = l1sub >= 6
            if DO_F:
                P.wait("g", "cc", 2)
                P.op("g", lambda g: g.engine_nop(), inc="sg")
                ar_sg = P.cnt["sg"]
                P.wait("sy", "sg", ar_sg)
                ar_ld = P.dma("sy", ars[:, :], ar_out.ap()[:, :], "lar")
                if "ctx" in taps:
                    P.dma("sy", tap_t["ctx"], ar_out.ap()[:, :], "stp")
                P.wait("v", "lar", ar_ld)
                P.drain("v")
                for b in range(B):
                    boff = b * 260
                    for hd in range(8):
                        bp = (hd % 2) * 64
                        cb = hd // 2
                        P.op("v", lambda v, bp=bp, cb=cb, boff=boff: v.reciprocal(
                            sca[bp:bp + 64, :], ars[bp:bp + 64, boff + cb * 65 + 64: boff + cb * 65 + 65]))
                        P.drain("v")
                        P.op("v", lambda v, bp=bp, cb=cb, boff=boff: v.tensor_scalar(
                            ctb[bp:bp + 64, boff + cb * 65: boff + cb * 65 + 64],
                            ars[bp:bp + 64, boff + cb * 65: boff + cb * 65 + 64],
                            sca[bp:bp + 64, 0:1], None, ALU.mult), inc="vv")
                        P.drain("v")
                ctb_vv = P.cnt["vv"]
                o1T = plb.enter_context(nc.sbuf_tensor("o1T", [128, 4 * TOK], BF16))
                lpo_ev = {0: 0, 1: 0}
                P.wait("t", "vv", ctb_vv)
                bl = range(B) if l1f >= 2 else []
                for b in bl:
                    boff = b * 260
                    for mk in range(8):
                        mm_ = b * 8 + mk
                        P.wait("t", "vv", lpo_ev[0])
                        P.wait("t", "vv", lpo_ev[1])
                        for hd in range(8):
                            bp = (hd % 2) * 64
                            cb = hd // 2
                            ps = lpo[hd % 2]
                            P.op("t", lambda t, hd=hd, mm_=mm_, bp=bp, cb=cb, boff=boff, ps=ps: t.matmul(
                                ps[:, cb * 64:(cb + 1) * 64],
                                qsT[bp:bp + 64, cb * TOK + mm_ * 128: cb * TOK + (mm_ + 1) * 128],
                                ctb[bp:bp + 64, boff + cb * 65: boff + cb * 65 + 64],
                                start=True, stop=True), inc="mm" if hd == 7 else None)
                        P.wait("v", "mm", P.cnt["mm"])
                        P.drain("v")
                        for hd in range(8):
                            P.op("v", lambda v, hd=hd: v.tensor_copy(
                                tmpd[:, hd * 64:(hd + 1) * 64],
                                lpo[hd % 2][:, (hd // 2) * 64:(hd // 2 + 1) * 64]), inc="vv")
                        lpo_ev[0] = lpo_ev[1] = P.cnt["vv"]
                        P.wait("t", "vv", P.cnt["vv"])
                        for ot in (range(4) if l1f >= 3 else []):
                            pst = lpt[ot % 2]
                            P.wait("t", "vv", pst5[id(pst)])
                            P.op("t", lambda t, ot=ot, pst=pst: t.transpose(
                                pst[:, :], tmpd[:, ot * 128:(ot + 1) * 128], ident_b[:, :]), inc="mm")
                            P.wait("v", "mm", P.cnt["mm"])
                            P.op("v", lambda v, ot=ot, mm_=mm_, pst=pst: v.tensor_copy(
                                o1T[:, ot * TOK + mm_ * 128: ot * TOK + (mm_ + 1) * 128], pst[:, :]), inc="vv")
                            pst5[id(pst)] = P.cnt["vv"]
                o1_vv = P.cnt["vv"]
                if "o1T" in taps:
                    P.wait("sy", "vv", o1_vv)
                    P.dma("sy", tap_t["o1T"], o1T[:, :], "stp")
                wo_sb = plb.enter_context(nc.sbuf_tensor("wos", [128, 4 * D], BF16))
                ml = range(NT) if l1f >= 4 else []
                P.wait("g", "mm", P.cnt["mm"])
                wo_lg = 0
                for t4 in range(4):
                    wo_lg = P.dma("g", wo_sb[:, t4 * D:(t4 + 1) * D], h["wo1"][t4], "lwo")
                P.wait("t", "lwo", wo_lg)
                P.wait("t", "vv", o1_vv)
                for m in ml:
                    for half in range(2):
                        ps = lps[half]
                        P.wait("t", "vv", lev[id(ps)])
                        for t4 in range(4):
                            P.op("t", lambda t, m=m, half=half, t4=t4, ps=ps: t.matmul(
                                ps[:, :], o1T[:, t4 * TOK + m * 128: t4 * TOK + (m + 1) * 128],
                                wo_sb[:, t4 * D + half * 512: t4 * D + (half + 1) * 512],
                                start=(t4 == 0), stop=(t4 == 3)), inc="mm" if t4 == 3 else None)
                        P.wait("v", "mm", P.cnt["mm"])
                        P.drain("v")
                        P.op("v", lambda v, m=m, half=half, ps=ps: v.tensor_add(
                            x_sb[:, m * D + half * 512: m * D + (half + 1) * 512],
                            x_sb[:, m * D + half * 512: m * D + (half + 1) * 512],
                            ps[:, :]), inc="vv")
                        lev[id(ps)] = P.cnt["vv"]
            P.flush("l1")
            plb.close()
            quiesce()

        if phases >= 3:
            P.wait("sy", "vv", P.cnt["vv"])
            for m in range(NT):
                P.dma("sy", y[m], x_sb[:, m * D:(m + 1) * D], "sty")
        else:
            P.wait("sy", "stx", P.cnt["stx"])
            for m in range(NT):
                P.dma("sy", y[m], xsp.ap()[:, m * D:(m + 1) * D], "sty")
        P.wait("sy", "sty", P.cnt["sty"])
        P.wait("sy", "stp", P.cnt["stp"])
        P.flush("out")
        pb.close()

    return nc, tap_t


def stage(inputs, core):
    g = {}
    x = np.asarray(inputs["L0_attn_x"])
    g["x_own"] = np.ascontiguousarray(x.reshape(B, 64, 128, D)[:, core::8]).reshape(NT, 128, D)
    for nm, w in [("wq", "L0_attn_Wq"), ("wk", "L0_attn_Wk"), ("wv", "L0_attn_Wv")]:
        g[nm] = np.ascontiguousarray(np.asarray(inputs[w]).reshape(8, 128, D))
    for nm, w in [("bq", "L0_attn_bq"), ("bk", "L0_attn_bk"), ("bv", "L0_attn_bv")]:
        g[nm] = np.ascontiguousarray(np.asarray(inputs[w]).reshape(8, 128, 1))
    rel = np.asarray(inputs["L0_attn_rel_emb"])
    p = np.arange(128)[:, None]
    m = np.arange(RELU)[None, :]
    idx = np.clip(127 + core * 128 - p + m, 0, 4094)
    g["relT"] = np.ascontiguousarray(rel[:, idx])
    g["chv"] = np.ascontiguousarray(rel[:, 4094:4095])
    g["gscale"] = np.ascontiguousarray(np.asarray(inputs["L0_ffn_norm_scale"]).reshape(8, 128, 1))
    g["gwT"] = np.ascontiguousarray(np.asarray(inputs["L0_ffn_gate_w"]).T.reshape(8, 128, NE))
    g["gb"] = np.ascontiguousarray(np.asarray(inputs["L0_ffn_gate_b"]).reshape(NE, 1))
    w1 = np.asarray(inputs["L0_ffn_w1"])
    g["w1g"] = np.ascontiguousarray(w1[:, 0::2, :].transpose(0, 2, 1)).reshape(NE, 8, 128, D)
    g["w1l"] = np.ascontiguousarray(w1[:, 1::2, :].transpose(0, 2, 1)).reshape(NE, 8, 128, D)
    g["w2T"] = np.ascontiguousarray(np.asarray(inputs["L0_ffn_w2"]).transpose(0, 2, 1)).reshape(NE, 8, 128, D)
    b1 = np.asarray(inputs["L0_ffn_b1"])
    g["b1g"] = np.ascontiguousarray(b1[:, 0::2].reshape(NE, 8, 128).transpose(2, 0, 1).reshape(128, NE * 8))
    g["b1l"] = np.ascontiguousarray(b1[:, 1::2].reshape(NE, 8, 128).transpose(2, 0, 1).reshape(128, NE * 8))
    g["b2"] = np.ascontiguousarray(np.asarray(inputs["L0_ffn_b2"]))
    g["wq1"] = np.ascontiguousarray(np.asarray(inputs["L1_attn_Wq"]).reshape(8, 128, 512))
    g["wk1"] = np.ascontiguousarray(np.asarray(inputs["L1_attn_Wk"]).reshape(8, 128, 512))
    g["wv1"] = np.ascontiguousarray(np.asarray(inputs["L1_attn_Wv"]).reshape(8, 128, 512))
    g["wo1"] = np.ascontiguousarray(np.asarray(inputs["L1_attn_Wo"]).reshape(4, 128, D))
    g["bo1"] = np.ascontiguousarray(np.asarray(inputs["L1_attn_bo"]).reshape(1, D))
    g["wf1"] = np.ascontiguousarray(np.asarray(inputs["L1_ffn_weight"]).reshape(8, 128, D))
    g["bf1"] = np.ascontiguousarray(np.asarray(inputs["L1_ffn_bias"]).reshape(1, D))
    g["ident_f"] = np.eye(128, dtype=np.float32)
    g["ident_b"] = np.eye(128, dtype=ml_dtypes.bfloat16)
    return g


def unstage(outs):
    out = np.empty((B, 64, 128, D), dtype=np.float32)
    for c in range(NCORES):
        out[:, c::8] = np.asarray(outs[c]).reshape(B, 8, 128, D)
    return out.reshape(B, 64 * 128, D)


def kernel(**inputs):
    from concourse.bass_utils import run_bass_kernel_spmd
    nc, _ = build(phases=4)
    in_maps = [stage(inputs, c) for c in range(NCORES)]
    res = run_bass_kernel_spmd(nc, in_maps, core_ids=list(range(NCORES)))
    return unstage([res.results[c]["y_own"] for c in range(NCORES)])
